# version 01 (final)
# baseline (speedup 1.0000x reference)
import numpy as np
import ml_dtypes
from contextlib import ExitStack

import concourse.bass as bass
import concourse.mybir as mybir
from concourse.bass import IndirectOffsetOnAxis
from concourse.bass_utils import run_bass_kernel_spmd

F32 = mybir.dt.float32
BF16 = mybir.dt.bfloat16
U32 = mybir.dt.uint32
AF = mybir.ActivationFunctionType
ALU = mybir.AluOpType
AX = mybir.AxisListType
BF = ml_dtypes.bfloat16

NCORE = 8
T = 4096
NT = T // 128
D = 1024

NEG = -3.0e38
ENGS = ("sync", "scalar", "vector", "tensor", "gpsimd")


class Prog:

    def __init__(self, name):
        self.nc = bass.Bass("TRN2", target_bir_lowering=False, debug=False)
        self.name = name
        self.cum = 0
        self.sched = {e: [] for e in ENGS}
        self.ctx = ExitStack()
        self._finalized = False

    def inp(self, name, shape, dt):
        return self.nc.dram_tensor(name, shape, dt, kind="ExternalInput").ap()

    def outp(self, name, shape, dt):
        return self.nc.dram_tensor(name, shape, dt, kind="ExternalOutput").ap()

    def sb(self, name, shape, dt):
        return self.ctx.enter_context(self.nc.sbuf_tensor(name, shape, dt))

    def ps(self, name, shape, dt=F32):
        return self.ctx.enter_context(self.nc.psum_tensor(name, shape, dt))

    def op(self, eng, fn, inc=1):
        self.sched[eng].append((self.cum, fn, inc))
        self.cum += inc

    def dma(self, out, in_):
        self.op("sync", lambda e, o=out, i=in_: e.dma_start(out=o, in_=i), 16)

    def finalize(self):
        assert not self._finalized
        self._finalized = True
        S = self.ctx.enter_context(self.nc.semaphore("S"))
        blk = self.ctx.enter_context(self.nc.Block())
        total = self.cum

        def emit(eng_name, engine):
            for pre, fn, inc in self.sched[eng_name]:
                engine.wait_ge(S, pre)
                fn(engine).then_inc(S, inc)

        @blk.sync
        def _(e):
            emit("sync", e)
            e.wait_ge(S, total)

        @blk.scalar
        def _(e):
            emit("scalar", e)

        @blk.vector
        def _(e):
            emit("vector", e)

        @blk.tensor
        def _(e):
            emit("tensor", e)

        @blk.gpsimd
        def _(e):
            emit("gpsimd", e)

        self.ctx.close()
        return self.nc


_cache = {}


def prog(key, builder, *args):
    if key not in _cache:
        _cache[key] = builder(*args)
    return _cache[key]


def run(nc, in_maps, out_names):
    res = run_bass_kernel_spmd(nc, in_maps, core_ids=list(range(NCORE)))
    return [{n: r[n] for n in out_names} for r in res.results]


EYE_BF = np.eye(128, dtype=BF)
EYE_F = np.eye(128, dtype=np.float32)
ONES_F = np.ones((1, 128), dtype=np.float32)
ONES_BF = np.ones((1, 128), dtype=BF)


def _bias_col(p, name, n_chunks):
    t = p.sb(name, [128, n_chunks], F32)
    return t


def ln_rows(p, x_sb, tmp_sb, stat, y_sb, Dw):
    negmean = stat[:, 0:1]
    var = stat[:, 1:2]
    p.op("vector", lambda e: e.reduce_sum(out=negmean, in_=x_sb[:], axis=AX.X, negate=True))
    p.op("scalar", lambda e: e.mul(out=negmean, in_=negmean, mul=1.0 / Dw))
    p.op("vector", lambda e: e.tensor_scalar(out=y_sb[:], in0=x_sb[:], scalar1=negmean, scalar2=None, op0=ALU.add))
    p.op("vector", lambda e: e.tensor_mul(out=tmp_sb[:], in0=y_sb[:], in1=y_sb[:]))
    p.op("vector", lambda e: e.reduce_sum(out=var, in_=tmp_sb[:], axis=AX.X))
    p.op("scalar", lambda e: e.activation(out=var, in_=var, func=AF.Ln, scale=1.0 / Dw, bias=p.eps[:, 0:1]))
    p.op("scalar", lambda e: e.activation(out=var, in_=var, func=AF.Exp, scale=-0.5))
    p.op("vector", lambda e: e.tensor_scalar(out=y_sb[:], in0=y_sb[:], scalar1=var, scalar2=None, op0=ALU.mult))


def add_eps(p):
    p.eps = p.sb("eps", [128, 1], F32)
    p.op("vector", lambda e: e.memset(p.eps[:], 1e-5))


def build_ln():
    p = Prog("ln")
    x = p.inp("x", [T, D], F32)
    eye = p.inp("eye", [128, 128], BF16)
    y = p.outp("y", [T, D], F32)
    yT = p.outp("yT", [D, T], BF16)
    ybf = p.outp("ybf", [T, D], BF16)
    add_eps(p)
    s_eye = p.sb("s_eye", [128, 128], BF16)
    p.dma(s_eye[:], eye)
    sx = p.sb("sx", [128, D], F32)
    st = p.sb("st", [128, D], F32)
    sy = p.sb("sy", [128, D], F32)
    syb = p.sb("syb", [128, D], BF16)
    stat = p.sb("stat", [128, 2], F32)
    str_ = p.sb("str", [128, 128], BF16)
    ptr = p.ps("ptr", [128, 128], BF16)
    for t in range(NT):
        r0 = t * 128
        p.dma(sx[:], x[r0:r0 + 128, :])
        ln_rows(p, sx, st, stat, sy, D)
        p.op("scalar", lambda e: e.copy(out=syb[:], in_=sy[:]))
        p.dma(y[r0:r0 + 128, :], sy[:])
        p.dma(ybf[r0:r0 + 128, :], syb[:])
        for c in range(8):
            p.op("tensor", lambda e, c=c: e.transpose(ptr[:], syb[:, c * 128:(c + 1) * 128], s_eye[:]))
            p.op("vector", lambda e: e.tensor_copy(out=str_[:], in_=ptr[:]))
            p.dma(yT[c * 128:(c + 1) * 128, r0:r0 + 128], str_[:])
    return p.finalize()


def build_gemm(key, K, N, out_f32, out_bf, out_T, bias, resid):
    p = Prog(key)
    KC = K // 128
    NC2 = (N + 511) // 512
    xT = p.inp("xT", [K, T], BF16)
    W = p.inp("W", [K, N], F32)
    eye = p.inp("eye", [128, 128], BF16)
    if bias:
        brow = p.inp("brow", [1, N], F32)
        onesf = p.inp("onesf", [1, 128], F32)
    if resid:
        res = p.inp("res", [T, N], F32)
    if out_f32:
        yf = p.outp("yf", [T, N], F32)
    if out_bf:
        ybf = p.outp("ybf", [T, N], BF16)
    if out_T:
        yT = p.outp("yT", [N, T], BF16)
    s_eye = p.sb("s_eye", [128, 128], BF16)
    p.dma(s_eye[:], eye)
    stream_x = K > 1024
    if stream_x:
        sxT = [p.sb(f"sxT{c}", [128, 128], BF16) for c in range(KC)]
    else:
        sxT = [p.sb(f"sxT{c}", [128, T], BF16) for c in range(KC)]
        for c in range(KC):
            p.dma(sxT[c][:], xT[c * 128:(c + 1) * 128, :])
    sW = [p.sb(f"sW{c}", [128, N], BF16) for c in range(KC)]
    sWf = p.sb("sWf", [128, N], F32)
    for c in range(KC):
        p.dma(sWf[:], W[c * 128:(c + 1) * 128, :])
        p.op("scalar", lambda e, c=c: e.copy(out=sW[c][:], in_=sWf[:]))
    if bias:
        sbr = p.sb("sbr", [1, N], F32)
        sones = p.sb("sones", [1, 128], F32)
        sbias = p.sb("sbias", [128, N], F32)
        pbias = p.ps("pbias", [128, 512], F32)
        p.dma(sbr[:], brow)
        p.dma(sones[:], onesf)
        for n in range(NC2):
            n0, n1 = n * 512, min((n + 1) * 512, N)
            p.op("tensor", lambda e, n0=n0, n1=n1: e.matmul(pbias[:, 0:n1 - n0], sones[:], sbr[:, n0:n1], start=True, stop=True))
            p.op("vector", lambda e, n0=n0, n1=n1: e.tensor_copy(out=sbias[:, n0:n1], in_=pbias[:, 0:n1 - n0]))
    pacc = p.ps("pacc", [128, 512], F32)
    sres = p.sb("sres", [128, N], F32) if resid else None
    syf = p.sb("syf", [128, N], F32)
    syb = p.sb("syb", [128, N], BF16) if (out_bf or out_T) else None
    if out_T:
        ptr = p.ps("ptr", [128, 128], BF16)
        strp = p.sb("strp", [128, 128], BF16)
    for t in range(NT):
        r0 = t * 128
        if resid:
            p.dma(sres[:], res[r0:r0 + 128, :])
        if stream_x:
            for c in range(KC):
                p.dma(sxT[c][:], xT[c * 128:(c + 1) * 128, r0:r0 + 128])
        for n in range(NC2):
            n0, n1 = n * 512, min((n + 1) * 512, N)
            for c in range(KC):
                xop = sxT[c][:] if stream_x else sxT[c][:, t * 128:(t + 1) * 128]
                p.op("tensor", lambda e, c=c, n0=n0, n1=n1, xop=xop: e.matmul(
                    pacc[:, 0:n1 - n0], xop, sW[c][:, n0:n1],
                    start=(c == 0), stop=(c == KC - 1)))
            if bias:
                p.op("vector", lambda e, n0=n0, n1=n1: e.tensor_tensor(
                    out=syf[:, n0:n1], in0=pacc[:, 0:n1 - n0], in1=sbias[:, n0:n1], op=ALU.add))
            else:
                p.op("vector", lambda e, n0=n0, n1=n1: e.tensor_copy(out=syf[:, n0:n1], in_=pacc[:, 0:n1 - n0]))
        if resid:
            p.op("vector", lambda e: e.tensor_add(out=syf[:], in0=syf[:], in1=sres[:]))
        if out_bf or out_T:
            p.op("scalar", lambda e: e.copy(out=syb[:], in_=syf[:]))
        if out_f32:
            p.dma(yf[r0:r0 + 128, :], syf[:])
        if out_bf:
            p.dma(ybf[r0:r0 + 128, :], syb[:])
        if out_T:
            for c in range(N // 128):
                p.op("tensor", lambda e, c=c: e.transpose(ptr[:], syb[:, c * 128:(c + 1) * 128], s_eye[:]))
                p.op("vector", lambda e: e.tensor_copy(out=strp[:], in_=ptr[:]))
                p.dma(yT[c * 128:(c + 1) * 128, r0:r0 + 128], strp[:])
    return p.finalize()


def build_l0e():
    p = Prog("l0e")
    ckv = p.inp("ckv", [T, 256], F32)
    kr = p.inp("kr", [T, 32], F32)
    kvx = p.outp("kvx", [T, 320], BF16)
    s1 = p.sb("s1", [128, 256], F32)
    s2 = p.sb("s2", [128, 32], F32)
    sa = p.sb("sa", [128, 32], F32)
    st = p.sb("st", [128, 32], F32)
    so = p.sb("so", [128, 320], BF16)
    sone = p.sb("sone", [128, 1], F32)
    p.op("vector", lambda e: e.memset(sone[:], 1.0))
    for t in range(NT):
        r0 = t * 128
        p.dma(s1[:], ckv[r0:r0 + 128, :])
        p.dma(s2[:], kr[r0:r0 + 128, :])
        p.op("scalar", lambda e: e.copy(out=so[:, 0:256], in_=s1[:]))
        p.op("scalar", lambda e: e.activation(out=sa[:], in_=s2[:], func=AF.Abs))
        p.op("scalar", lambda e: e.activation(out=sa[:], in_=sa[:], func=AF.Exp, scale=-1.0))
        p.op("scalar", lambda e: e.activation(out=sa[:], in_=sa[:], func=AF.Ln, bias=sone[:, 0:1]))
        p.op("vector", lambda e: e.tensor_scalar(out=st[:], in0=s2[:], scalar1=0.0, scalar2=None, op0=ALU.max))
        p.op("vector", lambda e: e.tensor_add(out=st[:], in0=st[:], in1=sa[:]))
        p.op("scalar", lambda e: e.copy(out=so[:, 256:288], in_=st[:]))
        p.op("scalar", lambda e: e.copy(out=so[:, 288:320], in_=st[:]))
        p.dma(kvx[r0:r0 + 128, :], so[:])
    return p.finalize()


def build_l0e2():
    p = Prog("l0e2")
    qr = p.inp("qr", [T, 256], F32)
    ctab = p.inp("ctab", [T, 128], F32)
    stab = p.inp("stab", [T, 128], F32)
    eye = p.inp("eye", [128, 128], BF16)
    qxet = p.outp("qxet", [NT, 64, 1024], BF16)
    s_eye = p.sb("s_eye", [128, 128], BF16)
    p.dma(s_eye[:], eye)
    sone = p.sb("sone", [128, 1], F32)
    p.op("vector", lambda e: e.memset(sone[:], 1.0))
    sqr = p.sb("sqr", [128, 256], F32)
    ssp = p.sb("ssp", [128, 256], F32)
    sa = p.sb("sa", [128, 256], F32)
    sc = p.sb("sc", [128, 128], F32)
    ss = p.sb("ss", [128, 128], F32)
    so1 = p.sb("so1", [128, 128], F32)
    so2 = p.sb("so2", [128, 128], F32)
    stmp = p.sb("stmp", [128, 128], F32)
    sxe = p.sb("sxe", [128, 512], BF16)
    sxef = p.sb("sxef", [128, 512], F32)
    ptr = p.ps("ptr", [128, 128], BF16)
    strp = p.sb("strp", [128, 128], BF16)
    for t in range(NT):
        r0 = t * 128
        p.dma(sqr[:], qr[r0:r0 + 128, :])
        p.dma(sc[:], ctab[r0:r0 + 128, :])
        p.dma(ss[:], stab[r0:r0 + 128, :])
        p.op("scalar", lambda e: e.activation(out=sa[:], in_=sqr[:], func=AF.Abs))
        p.op("scalar", lambda e: e.activation(out=sa[:], in_=sa[:], func=AF.Exp, scale=-1.0))
        p.op("scalar", lambda e: e.activation(out=sa[:], in_=sa[:], func=AF.Ln, bias=sone[:, 0:1]))
        p.op("vector", lambda e: e.tensor_scalar(out=ssp[:], in0=sqr[:], scalar1=0.0, scalar2=None, op0=ALU.max))
        p.op("vector", lambda e: e.tensor_add(out=ssp[:], in0=ssp[:], in1=sa[:]))
        mu1 = ssp[:].rearrange("p (h c) -> p h c", c=32)[:, :, 0:16]
        mu2 = ssp[:].rearrange("p (h c) -> p h c", c=32)[:, :, 16:32]
        c3 = sc[:].rearrange("p (h c) -> p h c", c=16)
        s3 = ss[:].rearrange("p (h c) -> p h c", c=16)
        o13 = so1[:].rearrange("p (h c) -> p h c", c=16)
        o23 = so2[:].rearrange("p (h c) -> p h c", c=16)
        t3 = stmp[:].rearrange("p (h c) -> p h c", c=16)
        p.op("vector", lambda e: e.tensor_tensor(out=o13, in0=mu1, in1=c3, op=ALU.mult))
        p.op("vector", lambda e: e.tensor_tensor(out=t3, in0=mu2, in1=s3, op=ALU.mult))
        p.op("vector", lambda e: e.tensor_sub(out=o13, in0=o13, in1=t3))
        p.op("vector", lambda e: e.tensor_tensor(out=o23, in0=mu1, in1=s3, op=ALU.mult))
        p.op("vector", lambda e: e.tensor_tensor(out=t3, in0=mu2, in1=c3, op=ALU.mult))
        p.op("vector", lambda e: e.tensor_add(out=o23, in0=o23, in1=t3))
        xe3 = sxef[:].rearrange("p (h c) -> p h c", c=64)
        p.op("vector", lambda e: e.tensor_copy(out=xe3[:, :, 0:16], in_=o13))
        p.op("vector", lambda e: e.tensor_copy(out=xe3[:, :, 16:32], in_=o23))
        p.op("vector", lambda e: e.tensor_copy(out=xe3[:, :, 32:48], in_=o23))
        p.op("vector", lambda e: e.tensor_scalar(out=xe3[:, :, 48:64], in0=o13, scalar1=-1.0, scalar2=None, op0=ALU.mult))
        p.op("scalar", lambda e: e.copy(out=sxe[:], in_=sxef[:]))
        for c in range(4):
            p.op("tensor", lambda e, c=c: e.transpose(ptr[:], sxe[:, c * 128:(c + 1) * 128], s_eye[:]))
            p.op("vector", lambda e: e.tensor_copy(out=strp[:], in_=ptr[:]))
            p.dma(qxet[t, :, (2 * c) * 128:(2 * c) * 128 + 128], strp[0:64, :])
            p.dma(qxet[t, :, (2 * c + 1) * 128:(2 * c + 1) * 128 + 128], strp[64:128, :])
    return p.finalize()


def build_l0tk():
    p = Prog("l0tk")
    qiT = p.inp("qiT", [256, T], BF16)
    kiT = p.inp("kiT", [64, T], BF16)
    idw = p.inp("idw", [1, 4], F32)
    onesf = p.inp("onesf", [1, 128], F32)
    mk0 = p.inp("mk0", [128, 128], F32)
    ma0 = p.inp("ma0", [128, 128], F32)
    eyef = p.inp("eyef", [128, 128], F32)
    idxT = p.outp("idxT", [96, T], U32)
    s_eyef = p.sb("s_eyef", [128, 128], F32)
    p.dma(s_eyef[:], eyef)
    sqiT = [p.sb(f"sqiT{c}", [128, T], BF16) for c in range(2)]
    skiT = p.sb("skiT", [128, T], BF16)
    p.dma(sqiT[0][:], qiT[0:128, :])
    p.dma(sqiT[1][:], qiT[128:256, :])
    p.dma(skiT[0:64, :], kiT)
    p.dma(skiT[64:128, :], kiT)
    stcol = p.sb("stcol", [128, 1], F32)
    sJm = p.sb("sJm", [128, 512], F32)
    sidw = p.sb("sidw", [1, 4], F32)
    sones = p.sb("sones", [1, 128], F32)
    sidwb = p.sb("sidwb", [128, 4], F32)
    pidw = p.ps("pidw", [128, 4], F32)
    p.dma(sidw[:], idw)
    p.dma(sones[:], onesf)
    p.op("tensor", lambda e: e.matmul(pidw[:], sones[:], sidw[:], start=True, stop=True))
    p.op("vector", lambda e: e.tensor_copy(out=sidwb[:], in_=pidw[:]))
    smk = p.sb("smk", [128, 128], F32)
    sma = p.sb("sma", [128, 128], F32)
    p.dma(smk[:], mk0)
    p.dma(sma[:], ma0)
    siota = p.sb("siota", [128, T], F32)
    p.op("gpsimd", lambda e: e.iota(out=siota[:], pattern=[[1, T]], base=0,
                                    channel_multiplier=0, allow_small_or_imprecise_dtypes=True))
    sI = p.sb("sI", [128, T], F32)
    sJ = p.sb("sJ", [128, 512], F32)
    psc = p.ps("psc", [128, 512], F32)
    stop8 = p.sb("stop8", [128, 8], F32)
    sidx96 = p.sb("sidx96", [128, 96], U32)
    sidxf = p.sb("sidxf", [128, 96], F32)
    pidx = p.ps("pidx", [128, 128], F32)
    sidxTf = p.sb("sidxTf", [96, 128], F32)
    sidxTu = p.sb("sidxTu", [96, 128], U32)
    for qt in range(NT):
        W = min((qt * 128 + 64 + 511) // 512 * 512, T)
        nch = W // 512
        for ch in range(nch):
            s0 = ch * 512
            for h in range(4):
                p.op("tensor", lambda e, h=h, qt=qt, s0=s0: e.matmul(
                    psc[:], sqiT[h // 2][:][(h % 2) * 64:(h % 2) * 64 + 64, qt * 128:(qt + 1) * 128],
                    skiT[(h % 2) * 64:(h % 2) * 64 + 64, s0:s0 + 512], start=True, stop=True))
                if h == 0:
                    p.op("vector", lambda e, s0=s0: e.tensor_scalar(
                        out=sI[:, s0:s0 + 512], in0=psc[:], scalar1=0.0, scalar2=sidwb[:, 0:1],
                        op0=ALU.max, op1=ALU.mult))
                else:
                    p.op("vector", lambda e, h=h, s0=s0: e.tensor_scalar(
                        out=sJ[:], in0=psc[:], scalar1=0.0, scalar2=sidwb[:, h:h + 1],
                        op0=ALU.max, op1=ALU.mult))
                    p.op("vector", lambda e, s0=s0: e.tensor_add(
                        out=sI[:, s0:s0 + 512], in0=sI[:, s0:s0 + 512], in1=sJ[:]))
            p.op("vector", lambda e, s0=s0: e.scalar_tensor_tensor(
                out=sJ[:], in0=sI[:, s0:s0 + 512], scalar=0.0, in1=siota[:, s0:s0 + 512],
                op0=ALU.is_equal, op1=ALU.mult))
            p.op("vector", lambda e, s0=s0: e.scalar_tensor_tensor(
                out=sI[:, s0:s0 + 512], in0=sJ[:], scalar=-(2.0 ** -40), in1=sI[:, s0:s0 + 512],
                op0=ALU.mult, op1=ALU.add))
        if qt == 0:
            p.op("vector", lambda e: e.tensor_mul(out=sI[:, 0:128], in0=sI[:, 0:128], in1=smk[:]))
            p.op("vector", lambda e: e.tensor_add(out=sI[:, 0:128], in0=sI[:, 0:128], in1=sma[:]))
            p.op("vector", lambda e: e.memset(sI[:, 128:512], NEG))
        else:
            p.op("gpsimd", lambda e, qt=qt: e.iota(out=stcol[:], pattern=[[1, 1]], base=qt * 128 - 64,
                                                   channel_multiplier=1, allow_small_or_imprecise_dtypes=True))
            for ch2 in range(W // 512):
                o0 = ch2 * 512
                p.op("vector", lambda e, o0=o0: e.tensor_scalar(
                    out=sJm[:], in0=siota[:, o0:o0 + 512], scalar1=stcol[:, 0:1], scalar2=NEG,
                    op0=ALU.is_gt, op1=ALU.mult))
                p.op("vector", lambda e, o0=o0: e.tensor_add(out=sI[:, o0:o0 + 512], in0=sI[:, o0:o0 + 512], in1=sJm[:]))
        p.op("gpsimd", lambda e, qt=qt: e.iota(out=sidx96[:, 0:64], pattern=[[1, 64]],
                                               base=qt * 128 - 63, channel_multiplier=1,
                                               allow_small_or_imprecise_dtypes=True))
        for r in range(4):
            p.op("vector", lambda e, W=W: e.max(out=stop8[:], in_=sI[:, 0:W]))
            p.op("vector", lambda e, r=r, W=W: e.max_index(out=sidx96[:, 64 + 8 * r:72 + 8 * r],
                                                           in_max=stop8[:], in_values=sI[:, 0:W]))
            p.op("vector", lambda e, W=W: e.match_replace(out=sI[:, 0:W], in_to_replace=stop8[:],
                                                          in_values=sI[:, 0:W], imm_value=NEG))
        if qt == 0:
            p.op("gpsimd", lambda e: e.iota(out=sidx96[0:64, :], pattern=[[1, 96]], base=0,
                                            channel_multiplier=0, allow_small_or_imprecise_dtypes=True))
        p.op("vector", lambda e: e.tensor_copy(out=sidxf[:], in_=sidx96[:]))
        p.op("tensor", lambda e: e.transpose(pidx[0:96, 0:128], sidxf[:], s_eyef[:]))
        p.op("vector", lambda e: e.tensor_copy(out=sidxTf[:], in_=pidx[0:96, 0:128]))
        p.op("vector", lambda e: e.tensor_copy(out=sidxTu[:], in_=sidxTf[:]))
        p.dma(idxT[:, qt * 128:(qt + 1) * 128], sidxTu[:])
    return p.finalize()


def build_l0at():
    p = Prog("l0at")
    kvx = p.inp("kvx", [T, 320], BF16)
    idxTs = p.inp("idxTs", [96, 512], U32)
    qcTs = p.inp("qcTs", [512, 512], BF16)
    qxets = p.inp("qxets", [4, 64, 1024], BF16)
    wukT = p.inp("wukT", [8, 128, 256], F32)
    cs_slot = p.inp("cs_slot", [96, 64], BF16)
    eye = p.inp("eye", [128, 128], BF16)
    olt = p.outp("olt", [4, 2, 128, 1024], BF16)
    SC = (64 + 32) ** -0.5
    s_eye = p.sb("s_eye", [128, 128], BF16)
    p.dma(s_eye[:], eye)
    s_cs = p.sb("s_cs", [96, 64], BF16)
    p.dma(s_cs[:], cs_slot)
    sidx = p.sb("sidx", [96, 512], U32)
    p.dma(sidx[:], idxTs)
    sqcT = [p.sb(f"sqcT{c}", [128, 512], BF16) for c in range(4)]
    for c in range(4):
        p.dma(sqcT[c][:], qcTs[c * 128:(c + 1) * 128, :])
    swukf = p.sb("swukf", [128, 256], F32)
    swuk = [p.sb(f"swuk{h}", [128, 256], BF16) for h in range(8)]
    for h in range(8):
        p.dma(swukf[:], wukT[h])
        p.op("scalar", lambda e, h=h: e.copy(out=swuk[h][:], in_=swukf[:]))
    sqxet = p.sb("sqxet", [64, 1024], BF16)
    sqabs = [p.sb(f"sqabs{c}", [128, 1024], BF16) for c in range(2)]
    pqa = p.ps("pqa", [128, 128], F32)
    sG = p.sb("sG", [96, 320], BF16)
    pg12 = p.ps("pg12", [128, 192], BF16)
    pg3 = p.ps("pg3", [96, 96], BF16)
    sGT = p.sb("sGT", [128, 192], BF16)
    sGT3 = p.sb("sGT3", [64, 96], BF16)
    psc = p.ps("psc", [8, 96], F32)
    ssc = p.sb("ssc", [8, 96], F32)
    sstat = p.sb("sstat", [8, 2], F32)
    sP = p.sb("sP", [8, 96], BF16)
    sPT = p.sb("sPT", [96, 8], BF16)
    po = p.ps("po", [128, 16], F32)
    solt = [p.sb(f"solt{c}", [128, 1024], BF16) for c in range(2)]
    for tt in range(4):
        p.dma(sqxet[:], qxets[tt])
        for h in range(8):
            for c in range(2):
                p.op("tensor", lambda e, h=h, c=c, tt=tt: e.matmul(
                    pqa[:], swuk[h][(h % 2) * 64:(h % 2) * 64 + 64, c * 128:(c + 1) * 128],
                    sqcT[h // 2][:][(h % 2) * 64:(h % 2) * 64 + 64, tt * 128:(tt + 1) * 128],
                    start=True, stop=True))
                p.op("scalar", lambda e, h=h, c=c: e.copy(out=sqabs[c][:, h * 128:(h + 1) * 128], in_=pqa[:]))
        for dt_ in range(128):
            tok = tt * 128 + dt_
            p.op("gpsimd", lambda e, tok=tok: e.indirect_dma_start(
                out=sG[:], out_offset=None, in_=kvx[0:96, :],
                in_offset=IndirectOffsetOnAxis(ap=sidx[:, tok:tok + 1], axis=0)), 16)
            p.op("vector", lambda e: e.tensor_tensor(out=sG[:, 256:320], in0=sG[:, 256:320], in1=s_cs[:], op=ALU.mult))
            p.op("tensor", lambda e: e.transpose(pg12[:, 0:96], sG[:, 0:128], s_eye[0:96, 0:96]))
            p.op("tensor", lambda e: e.transpose(pg12[:, 96:192], sG[:, 128:256], s_eye[0:96, 0:96]))
            p.op("tensor", lambda e: e.transpose(pg3[0:64, 0:96], sG[:, 256:320], s_eye[0:96, 0:96]))
            p.op("vector", lambda e: e.tensor_copy(out=sGT[:], in_=pg12[:]))
            p.op("vector", lambda e: e.tensor_copy(out=sGT3[:], in_=pg3[0:64, 0:96]))
            qa1 = sqabs[0][:].rearrange("p (h t) -> p t h", t=128)
            qa2 = sqabs[1][:].rearrange("p (h t) -> p t h", t=128)
            qx = sqxet[:].rearrange("p (h t) -> p t h", t=128)
            p.op("tensor", lambda e, dt_=dt_: e.matmul(psc[:], qa1[:, dt_, :], sGT[:, 0:96], start=True, stop=False))
            p.op("tensor", lambda e, dt_=dt_: e.matmul(psc[:], qa2[:, dt_, :], sGT[:, 96:192], start=False, stop=False))
            p.op("tensor", lambda e, dt_=dt_: e.matmul(psc[:], qx[:, dt_, :], sGT3[:], start=False, stop=True))
            p.op("vector", lambda e: e.reduce_max(out=sstat[:, 0:1], in_=psc[:], axis=AX.X, negate=True))
            p.op("scalar", lambda e: e.mul(out=sstat[:, 0:1], in_=sstat[:, 0:1], mul=SC))
            p.op("scalar", lambda e: e.activation(out=ssc[:], in_=psc[:], func=AF.Exp, scale=SC, bias=sstat[:, 0:1]))
            p.op("vector", lambda e: e.reduce_sum(out=sstat[:, 1:2], in_=ssc[:], axis=AX.X))
            p.op("vector", lambda e: e.reciprocal(out=sstat[:, 1:2], in_=sstat[:, 1:2]))
            p.op("vector", lambda e: e.tensor_scalar(out=ssc[:], in0=ssc[:], scalar1=sstat[:, 1:2], scalar2=None, op0=ALU.mult))
            p.op("scalar", lambda e: e.copy(out=sP[:], in_=ssc[:]))
            p.op("tensor", lambda e: e.transpose(pg3[:, 0:8], sP[:], s_eye[0:8, 0:8]))
            p.op("vector", lambda e: e.tensor_copy(out=sPT[:], in_=pg3[0:96, 0:8]))
            p.op("tensor", lambda e: e.matmul(po[:, 0:8], sG[:, 0:128], sPT[:], start=True, stop=True))
            p.op("tensor", lambda e: e.matmul(po[:, 8:16], sG[:, 128:256], sPT[:], start=True, stop=True))
            o1 = solt[0][:].rearrange("p (h t) -> p t h", t=128)
            o2 = solt[1][:].rearrange("p (h t) -> p t h", t=128)
            p.op("scalar", lambda e, dt_=dt_: e.copy(out=o1[:, dt_, :], in_=po[:, 0:8]))
            p.op("scalar", lambda e, dt_=dt_: e.copy(out=o2[:, dt_, :], in_=po[:, 8:16]))
        p.dma(olt[tt, 0], solt[0][:])
        p.dma(olt[tt, 1], solt[1][:])
    return p.finalize()


def build_l0out():
    p = Prog("l0out")
    olt = p.inp("olt", [NT, 2, 128, 1024], BF16)
    wuv = p.inp("wuv", [2, 8, 128, 64], F32)
    wout = p.inp("wout", [512, D], F32)
    x0 = p.inp("x0", [T, D], F32)
    x1 = p.outp("x1", [T, D], F32)
    swuvf = p.sb("swuvf", [128, 64], F32)
    swuv = [[p.sb(f"swuv{c}_{h}", [128, 64], BF16) for h in range(8)] for c in range(2)]
    for c in range(2):
        for h in range(8):
            p.dma(swuvf[:], wuv[c, h])
            p.op("scalar", lambda e, c=c, h=h: e.copy(out=swuv[c][h][:], in_=swuvf[:]))
    swof = p.sb("swof", [128, D], F32)
    swo = [p.sb(f"swo{c}", [128, D], BF16) for c in range(4)]
    for c in range(4):
        p.dma(swof[:], wout[c * 128:(c + 1) * 128, :])
        p.op("scalar", lambda e, c=c: e.copy(out=swo[c][:], in_=swof[:]))
    sol = [p.sb(f"sol{c}", [128, 1024], BF16) for c in range(2)]
    pyt = p.ps("pyt", [64, 128], F32)
    syt = [p.sb(f"syt{c}", [128, 128], BF16) for c in range(4)]
    pac = p.ps("pac", [128, 512], F32)
    sx = p.sb("sx", [128, D], F32)
    sy = p.sb("sy", [128, D], F32)
    for tt in range(NT):
        r0 = tt * 128
        p.dma(sol[0][:], olt[tt, 0])
        p.dma(sol[1][:], olt[tt, 1])
        p.dma(sx[:], x0[r0:r0 + 128, :])
        for h in range(8):
            for c in range(2):
                p.op("tensor", lambda e, h=h, c=c: e.matmul(
                    pyt[:], swuv[c][h][:], sol[c][:, h * 128:(h + 1) * 128],
                    start=(c == 0), stop=(c == 1)))
            p.op("scalar", lambda e, h=h: e.copy(out=syt[h // 2][(h % 2) * 64:(h % 2) * 64 + 64, :], in_=pyt[:]))
        for n in range(2):
            for c in range(4):
                p.op("tensor", lambda e, n=n, c=c: e.matmul(
                    pac[:], syt[c][:], swo[c][:, n * 512:(n + 1) * 512],
                    start=(c == 0), stop=(c == 3)))
            p.op("vector", lambda e, n=n: e.tensor_tensor(
                out=sy[:, n * 512:(n + 1) * 512], in0=pac[:], in1=sx[:, n * 512:(n + 1) * 512], op=ALU.add))
        p.dma(x1[r0:r0 + 128, :], sy[:])
    return p.finalize()


def build_moe_a():
    p = Prog("moea")
    yT = p.inp("yT", [D, T], BF16)
    yb = p.inp("yb", [T, D], BF16)
    phiT = p.inp("phiT", [D, 2048], F32)
    brow = p.inp("brow", [1, 2048], F32)
    onesf = p.inp("onesf", [1, 128], F32)
    logits = p.outp("logits", [T, 2048], F32)
    xpart = p.outp("xpart", [2048, D], F32)
    syT = [p.sb(f"syT{c}", [128, 512], BF16) for c in range(8)]
    sphf = p.sb("sphf", [128, 2048], F32)
    sph = [p.sb(f"sph{c}", [128, 2048], BF16) for c in range(8)]
    for c in range(8):
        p.dma(sphf[:], phiT[c * 128:(c + 1) * 128, :])
        p.op("scalar", lambda e, c=c: e.copy(out=sph[c][:], in_=sphf[:]))
    sbr = p.sb("sbr", [1, 2048], F32)
    sones = p.sb("sones", [1, 128], F32)
    sbias = p.sb("sbias", [128, 2048], F32)
    pb = p.ps("pb", [128, 512], F32)
    p.dma(sbr[:], brow)
    p.dma(sones[:], onesf)
    for n in range(4):
        p.op("tensor", lambda e, n=n: e.matmul(pb[:], sones[:], sbr[:, n * 512:(n + 1) * 512], start=True, stop=True))
        p.op("vector", lambda e, n=n: e.tensor_copy(out=sbias[:, n * 512:(n + 1) * 512], in_=pb[:]))
    plg = p.ps("plg", [128, 512], F32)
    slg = p.sb("slg", [128, 2048], F32)
    sstat = p.sb("sstat", [128, 64], F32)
    sD = [p.sb(f"sD{i}", [128, 2048], BF16) for i in range(4)]
    syb = [p.sb(f"syb{i}", [128, D], BF16) for i in range(4)]
    px1 = p.ps("px1", [128, 512], F32)
    px2 = p.ps("px2", [128, 512], F32)
    sxacc = [p.sb(f"sxacc{m}", [128, D], F32) for m in range(16)]
    for g in range(8):
        for c in range(8):
            p.dma(syT[c][:], yT[c * 128:(c + 1) * 128, g * 512:(g + 1) * 512])
        for i in range(4):
            tt = g * 4 + i
            r0 = tt * 128
            p.dma(syb[i][:], yb[r0:r0 + 128, :])
            for n in range(4):
                for c in range(8):
                    p.op("tensor", lambda e, n=n, c=c, i=i: e.matmul(
                        plg[:], syT[c][:, i * 128:(i + 1) * 128], sph[c][:, n * 512:(n + 1) * 512],
                        start=(c == 0), stop=(c == 7)))
                p.op("vector", lambda e, n=n: e.tensor_tensor(
                    out=slg[:, n * 512:(n + 1) * 512], in0=plg[:], in1=sbias[:, n * 512:(n + 1) * 512], op=ALU.add))
            p.dma(logits[r0:r0 + 128, :], slg[:])
            l3 = slg[:].rearrange("p (e q) -> p e q", q=32)
            st3 = sstat[:].rearrange("p (e o) -> p e o", o=1)
            p.op("vector", lambda e: e.tensor_reduce(out=sstat[:], in_=l3, axis=AX.X, op=ALU.max, negate=True))
            p.op("vector", lambda e: e.tensor_tensor(out=l3, in0=l3, in1=st3.to_broadcast([128, 64, 32]), op=ALU.add))
            p.op("scalar", lambda e: e.activation(out=slg[:], in_=slg[:], func=AF.Exp))
            p.op("vector", lambda e: e.tensor_reduce(out=sstat[:], in_=l3, axis=AX.X, op=ALU.add))
            p.op("vector", lambda e: e.reciprocal(out=sstat[:], in_=sstat[:]))
            p.op("vector", lambda e: e.tensor_tensor(out=l3, in0=l3, in1=st3.to_broadcast([128, 64, 32]), op=ALU.mult))
            p.op("scalar", lambda e, i=i: e.copy(out=sD[i][:], in_=slg[:]))
        for m in range(16):
            for i in range(4):
                p.op("tensor", lambda e, m=m, i=i: e.matmul(
                    px1[:], sD[i][:, m * 128:(m + 1) * 128], syb[i][:, 0:512], start=(i == 0), stop=(i == 3)))
            for i in range(4):
                p.op("tensor", lambda e, m=m, i=i: e.matmul(
                    px2[:], sD[i][:, m * 128:(m + 1) * 128], syb[i][:, 512:1024], start=(i == 0), stop=(i == 3)))
            if g == 0:
                p.op("vector", lambda e, m=m: e.tensor_copy(out=sxacc[m][:, 0:512], in_=px1[:]))
                p.op("vector", lambda e, m=m: e.tensor_copy(out=sxacc[m][:, 512:1024], in_=px2[:]))
            else:
                p.op("vector", lambda e, m=m: e.tensor_add(out=sxacc[m][:, 0:512], in0=sxacc[m][:, 0:512], in1=px1[:]))
                p.op("vector", lambda e, m=m: e.tensor_add(out=sxacc[m][:, 512:1024], in0=sxacc[m][:, 512:1024], in1=px2[:]))
    for m in range(16):
        p.dma(xpart[m * 128:(m + 1) * 128, :], sxacc[m][:])
    return p.finalize()


def build_moe_b():
    p = Prog("moeb")
    xps = [p.inp(f"xp{i}", [2048, D], F32) for i in range(8)]
    logits = p.inp("logits", [T, 2048], F32)
    ew = p.inp("ew", [64, D, D], F32)
    ebrep = p.inp("ebrep", [2048, D], F32)
    x1 = p.inp("x1", [T, D], F32)
    eye = p.inp("eye", [128, 128], BF16)
    x2 = p.outp("x2", [T, D], F32)
    add_eps(p)
    s_eye = p.sb("s_eye", [128, 128], BF16)
    p.dma(s_eye[:], eye)
    sxa = p.sb("sxa", [128, D], F32)
    sxb = p.sb("sxb", [128, D], F32)
    stmp = p.sb("stmp", [128, D], F32)
    sstat = p.sb("sstat", [128, 2], F32)
    sxt = p.sb("sxt", [128, D], F32)
    sxtb = p.sb("sxtb", [128, D], BF16)
    ptr = p.ps("ptr", [128, 128], BF16)
    sXT = [p.sb(f"sXT{c}", [128, 2048], BF16) for c in range(8)]
    for m in range(16):
        r0 = m * 128
        p.dma(sxa[:], xps[0][r0:r0 + 128, :])
        for i in range(1, 8):
            p.dma(sxb[:], xps[i][r0:r0 + 128, :])
            p.op("vector", lambda e: e.tensor_add(out=sxa[:], in0=sxa[:], in1=sxb[:]))
        p.op("vector", lambda e: e.tensor_scalar(out=sxa[:], in0=sxa[:], scalar1=-33000.0, scalar2=65000.0, op0=ALU.max, op1=ALU.min))
        ln_rows(p, sxa, stmp, sstat, sxt, D)
        p.op("scalar", lambda e: e.activation(out=sxt[:], in_=sxt[:], func=AF.Tanh))
        p.op("scalar", lambda e: e.copy(out=sxtb[:], in_=sxt[:]))
        for c in range(8):
            p.op("tensor", lambda e, c=c: e.transpose(ptr[:], sxtb[:, c * 128:(c + 1) * 128], s_eye[:]))
            p.op("vector", lambda e, c=c, m=m: e.tensor_copy(out=sXT[c][:, m * 128:(m + 1) * 128], in_=ptr[:]))
    sewf = p.sb("sewf", [128, D], F32)
    sew = p.sb("sew", [128, D], BF16)
    pyt = p.ps("pyt", [32, 512], F32)
    pyt2 = p.ps("pyt2", [32, 512], F32)
    seb = p.sb("seb", [32, D], F32)
    syt_f = p.sb("syt_f", [32, D], F32)
    sytb = p.sb("sytb", [32, D], BF16)
    sYT = [p.sb(f"sYT{i}", [128, D], BF16) for i in range(16)]
    for n in range(64):
        p.dma(seb[:], ebrep[n * 32:(n + 1) * 32, :])
        for c in range(8):
            p.dma(sewf[:], ew[n, c * 128:(c + 1) * 128, :])
            p.op("scalar", lambda e: e.copy(out=sew[:], in_=sewf[:]))
            p.op("tensor", lambda e, n=n, c=c: e.matmul(
                pyt[:], sXT[c][:, n * 32:(n + 1) * 32], sew[:, 0:512], start=(c == 0), stop=(c == 7)))
            p.op("tensor", lambda e, n=n, c=c: e.matmul(
                pyt2[:], sXT[c][:, n * 32:(n + 1) * 32], sew[:, 512:1024], start=(c == 0), stop=(c == 7)))
        p.op("vector", lambda e: e.tensor_tensor(out=syt_f[:, 0:512], in0=pyt[:], in1=seb[:, 0:512], op=ALU.add))
        p.op("vector", lambda e: e.tensor_tensor(out=syt_f[:, 512:1024], in0=pyt2[:], in1=seb[:, 512:1024], op=ALU.add))
        p.op("scalar", lambda e: e.copy(out=sytb[:], in_=syt_f[:]))
        p.dma(sYT[n // 4][(n % 4) * 32:(n % 4) * 32 + 32, :], sytb[:])
    slg = p.sb("slg", [128, 2048], F32)
    sstat2 = p.sb("sstat2", [128, 32], F32)
    sCb = p.sb("sCb", [128, 2048], BF16)
    pc = p.ps("pc", [128, 128], BF16)
    sCT = [p.sb(f"sCT{c}", [128, 128], BF16) for c in range(16)]
    pyy = p.ps("pyy", [128, 512], F32)
    sx1 = p.sb("sx1", [128, D], F32)
    syy = p.sb("syy", [128, D], F32)
    for tt in range(NT):
        r0 = tt * 128
        p.dma(slg[:], logits[r0:r0 + 128, :])
        p.dma(sx1[:], x1[r0:r0 + 128, :])
        l3 = slg[:].rearrange("p (e q) -> p q e", q=32)
        st3 = sstat2[:].rearrange("p (q o) -> p q o", o=1)
        p.op("vector", lambda e: e.tensor_reduce(out=sstat2[:], in_=l3, axis=AX.X, op=ALU.max, negate=True))
        p.op("vector", lambda e: e.tensor_tensor(out=l3, in0=l3, in1=st3.to_broadcast([128, 32, 64]), op=ALU.add))
        p.op("scalar", lambda e: e.activation(out=slg[:], in_=slg[:], func=AF.Exp))
        p.op("vector", lambda e: e.tensor_reduce(out=sstat2[:], in_=l3, axis=AX.X, op=ALU.add))
        p.op("vector", lambda e: e.reciprocal(out=sstat2[:], in_=sstat2[:]))
        p.op("vector", lambda e: e.tensor_tensor(out=l3, in0=l3, in1=st3.to_broadcast([128, 32, 64]), op=ALU.mult))
        p.op("scalar", lambda e: e.copy(out=sCb[:], in_=slg[:]))
        for c in range(16):
            p.op("tensor", lambda e, c=c: e.transpose(pc[:], sCb[:, c * 128:(c + 1) * 128], s_eye[:]))
            p.op("vector", lambda e, c=c: e.tensor_copy(out=sCT[c][:], in_=pc[:]))
        for n in range(2):
            for c in range(16):
                p.op("tensor", lambda e, n=n, c=c: e.matmul(
                    pyy[:], sCT[c][:], sYT[c][:, n * 512:(n + 1) * 512], start=(c == 0), stop=(c == 15)))
            p.op("vector", lambda e, n=n: e.tensor_tensor(
                out=syy[:, n * 512:(n + 1) * 512], in0=pyy[:], in1=sx1[:, n * 512:(n + 1) * 512], op=ALU.add))
        p.dma(x2[r0:r0 + 128, :], syy[:])
    return p.finalize()


def build_attn_full():
    p = Prog("attnfull")
    qT = p.inp("qT", [256, T], BF16)
    kT = p.inp("kT", [256, T], BF16)
    V = p.inp("V", [T, 256], BF16)
    eye = p.inp("eye", [128, 128], BF16)
    aoT = p.outp("aoT", [256, T], BF16)
    SCALE = 0.125
    s_eye = p.sb("s_eye", [128, 128], BF16)
    p.dma(s_eye[:], eye)
    skT = [p.sb(f"skT{c}", [128, T], BF16) for c in range(2)]
    for c in range(2):
        p.dma(skT[c][:], kT[c * 128:(c + 1) * 128, :])
    sV = [p.sb(f"sV{i}", [128, 256], BF16) for i in range(NT)]
    for i in range(NT):
        p.dma(sV[i][:], V[i * 128:(i + 1) * 128, :])
    sq = [p.sb(f"sq{c}", [128, 128], BF16) for c in range(2)]
    siota = p.sb("siota", [128, T], F32)
    p.op("gpsimd", lambda e: e.iota(out=siota[:], pattern=[[1, T]], base=0,
                                    channel_multiplier=0, allow_small_or_imprecise_dtypes=True))
    stcol = p.sb("stcol", [128, 1], F32)
    smt = p.sb("smt", [128, 512], F32)
    psc = p.ps("psc", [128, 512], F32)
    sS = p.sb("sS", [128, T], F32)
    sstat = p.sb("sstat", [128, 2], F32)
    sPb = p.sb("sPb", [128, T], BF16)
    ppt = p.ps("ppt", [128, 128], BF16)
    sPT = p.sb("sPT", [128, 128], BF16)
    pout = p.ps("pout", [64, 128], F32)
    sout = p.sb("sout", [64, 128], BF16)
    for qt in range(NT):
        nch = qt // 4 + 1
        W = nch * 512
        p.dma(sq[0][:], qT[0:128, qt * 128:(qt + 1) * 128])
        p.dma(sq[1][:], qT[128:256, qt * 128:(qt + 1) * 128])
        p.op("gpsimd", lambda e, qt=qt: e.iota(out=stcol[:], pattern=[[1, 1]], base=qt * 128,
                                               channel_multiplier=1, allow_small_or_imprecise_dtypes=True))
        for h in range(4):
            qs = sq[h // 2][:][(h % 2) * 64:(h % 2) * 64 + 64, :]
            ks = skT[h // 2][:][(h % 2) * 64:(h % 2) * 64 + 64, :]
            for ch in range(nch):
                s0 = ch * 512
                p.op("tensor", lambda e, qs=qs, ks=ks, s0=s0: e.matmul(
                    psc[:], qs, ks[:, s0:s0 + 512], start=True, stop=True))
                p.op("vector", lambda e, s0=s0: e.tensor_copy(out=sS[:, s0:s0 + 512], in_=psc[:]))
            p.op("vector", lambda e, W=W: e.tensor_scalar(
                out=smt[:], in0=siota[:, W - 512:W], scalar1=stcol[:, 0:1], scalar2=NEG,
                op0=ALU.is_gt, op1=ALU.mult))
            p.op("vector", lambda e, W=W: e.tensor_add(out=sS[:, W - 512:W], in0=sS[:, W - 512:W], in1=smt[:]))
            p.op("vector", lambda e, W=W: e.reduce_max(out=sstat[:, 0:1], in_=sS[:, 0:W], axis=AX.X, negate=True))
            p.op("scalar", lambda e: e.mul(out=sstat[:, 0:1], in_=sstat[:, 0:1], mul=SCALE))
            p.op("scalar", lambda e, W=W: e.activation(out=sS[:, 0:W], in_=sS[:, 0:W], func=AF.Exp, scale=SCALE, bias=sstat[:, 0:1]))
            p.op("vector", lambda e, W=W: e.reduce_sum(out=sstat[:, 1:2], in_=sS[:, 0:W], axis=AX.X))
            p.op("vector", lambda e: e.reciprocal(out=sstat[:, 1:2], in_=sstat[:, 1:2]))
            p.op("vector", lambda e, W=W: e.tensor_scalar(out=sS[:, 0:W], in0=sS[:, 0:W], scalar1=sstat[:, 1:2], scalar2=None, op0=ALU.mult))
            p.op("scalar", lambda e, W=W: e.copy(out=sPb[:, 0:W], in_=sS[:, 0:W]))
            nsc = qt + 1
            for sc_ in range(nsc):
                p.op("tensor", lambda e, sc_=sc_: e.transpose(ppt[:], sPb[:, sc_ * 128:(sc_ + 1) * 128], s_eye[:]))
                p.op("vector", lambda e: e.tensor_copy(out=sPT[:], in_=ppt[:]))
                p.op("tensor", lambda e, sc_=sc_, h=h, nsc=nsc: e.matmul(
                    pout[:], sV[sc_][:, h * 64:(h + 1) * 64], sPT[:],
                    start=(sc_ == 0), stop=(sc_ == nsc - 1)))
            p.op("scalar", lambda e: e.copy(out=sout[:], in_=pout[:]))
            p.dma(aoT[h * 64:(h + 1) * 64, qt * 128:(qt + 1) * 128], sout[:])
    return p.finalize()


def build_attn_loc():
    p = Prog("attnloc")
    qT = p.inp("qT", [512, T], BF16)
    kT = p.inp("kT", [512, T], BF16)
    V = p.inp("V", [T, 512], BF16)
    masks = p.inp("masks", [128, 2048], F32)
    eye = p.inp("eye", [128, 128], BF16)
    aoT = p.outp("aoT", [512, T], BF16)
    SCALE = 8.0
    s_eye = p.sb("s_eye", [128, 128], BF16)
    p.dma(s_eye[:], eye)
    smask = p.sb("smask", [128, 2048], F32)
    p.dma(smask[:], masks)
    sq = [p.sb(f"sq{c}", [128, 128], BF16) for c in range(4)]
    skw = [p.sb(f"skw{c}", [128, 512], BF16) for c in range(4)]
    svw = [p.sb(f"svw{c}", [128, 512], BF16) for c in range(4)]
    psc = p.ps("psc", [128, 512], F32)
    sS = p.sb("sS", [128, 512], F32)
    sstat = p.sb("sstat", [128, 2], F32)
    sPb = p.sb("sPb", [128, 512], BF16)
    ppt = p.ps("ppt", [128, 128], BF16)
    sPT = p.sb("sPT", [128, 128], BF16)
    pout = p.ps("pout", [64, 128], F32)
    sout = p.sb("sout", [64, 128], BF16)
    for qt in range(NT):
        wb = qt // 2
        w0 = wb * 256 - 256
        for c in range(4):
            p.dma(sq[c][:], qT[c * 128:(c + 1) * 128, qt * 128:(qt + 1) * 128])
        if wb == 0:
            for c in range(4):
                p.op("vector", lambda e, c=c: e.memset(skw[c][:, 0:256], 0.0))
                p.dma(skw[c][:, 256:512], kT[c * 128:(c + 1) * 128, 0:256])
            p.op("vector", lambda e: e.memset(svw[0][:], 0.0))
            p.op("vector", lambda e: e.memset(svw[1][:], 0.0))
            p.dma(svw[2][:], V[0:128, :])
            p.dma(svw[3][:], V[128:256, :])
        else:
            for c in range(4):
                p.dma(skw[c][:], kT[c * 128:(c + 1) * 128, w0:w0 + 512])
            for c in range(4):
                p.dma(svw[c][:], V[w0 + c * 128:w0 + (c + 1) * 128, :])
        mvar = (0 if wb == 0 else 2) + (qt % 2)
        for h in range(8):
            qs = sq[h // 2][:][(h % 2) * 64:(h % 2) * 64 + 64, :]
            p.op("tensor", lambda e, qs=qs, h=h: e.matmul(
                psc[:], qs, skw[h // 2][:][(h % 2) * 64:(h % 2) * 64 + 64, :], start=True, stop=True))
            p.op("vector", lambda e, mvar=mvar: e.tensor_tensor(
                out=sS[:], in0=psc[:], in1=smask[:, mvar * 512:(mvar + 1) * 512], op=ALU.add))
            p.op("vector", lambda e: e.reduce_max(out=sstat[:, 0:1], in_=sS[:], axis=AX.X, negate=True))
            p.op("scalar", lambda e: e.mul(out=sstat[:, 0:1], in_=sstat[:, 0:1], mul=SCALE))
            p.op("scalar", lambda e: e.activation(out=sS[:], in_=sS[:], func=AF.Exp, scale=SCALE, bias=sstat[:, 0:1]))
            p.op("vector", lambda e: e.reduce_sum(out=sstat[:, 1:2], in_=sS[:], axis=AX.X))
            p.op("vector", lambda e: e.reciprocal(out=sstat[:, 1:2], in_=sstat[:, 1:2]))
            p.op("vector", lambda e: e.tensor_scalar(out=sS[:], in0=sS[:], scalar1=sstat[:, 1:2], scalar2=None, op0=ALU.mult))
            p.op("scalar", lambda e: e.copy(out=sPb[:], in_=sS[:]))
            for sc_ in range(4):
                p.op("tensor", lambda e, sc_=sc_: e.transpose(ppt[:], sPb[:, sc_ * 128:(sc_ + 1) * 128], s_eye[:]))
                p.op("vector", lambda e: e.tensor_copy(out=sPT[:], in_=ppt[:]))
                p.op("tensor", lambda e, sc_=sc_, h=h: e.matmul(
                    pout[:], svw[sc_][:, h * 64:(h + 1) * 64], sPT[:],
                    start=(sc_ == 0), stop=(sc_ == 3)))
            p.op("scalar", lambda e: e.copy(out=sout[:], in_=pout[:]))
            p.dma(aoT[h * 64:(h + 1) * 64, qt * 128:(qt + 1) * 128], sout[:])
    return p.finalize()


def build_qkv():
    p = Prog("qkv")
    x = p.inp("x", [T, D], F32)
    Wq = p.inp("Wq", [D, 3072], F32)
    qsrow = p.inp("qsrow", [1, D], F32)
    ksrow = p.inp("ksrow", [1, D], F32)
    csq = p.inp("csq", [T, D], F32)
    ssq = p.inp("ssq", [T, D], F32)
    csk = p.inp("csk", [T, D], F32)
    ssk = p.inp("ssk", [T, D], F32)
    onesf = p.inp("onesf", [1, 128], F32)
    eye = p.inp("eye", [128, 128], BF16)
    qT = p.outp("qT", [D, T], BF16)
    kT = p.outp("kT", [D, T], BF16)
    V = p.outp("V", [T, D], BF16)
    s_eye = p.sb("s_eye", [128, 128], BF16)
    p.dma(s_eye[:], eye)
    seps12 = p.sb("seps12", [128, 1], F32)
    p.op("vector", lambda e: e.memset(seps12[:], 1e-12))
    sWf = p.sb("sWf", [128, 3072], F32)
    sW = [p.sb(f"sW{c}", [128, 3072], BF16) for c in range(8)]
    for c in range(8):
        p.dma(sWf[:], Wq[c * 128:(c + 1) * 128, :])
        p.op("scalar", lambda e, c=c: e.copy(out=sW[c][:], in_=sWf[:]))
    sones = p.sb("sones", [1, 128], F32)
    p.dma(sones[:], onesf)
    srow = p.sb("srow", [1, D], F32)
    sqs = p.sb("sqs", [128, D], F32)
    sks = p.sb("sks", [128, D], F32)
    pb = p.ps("pb", [128, 512], F32)
    for nm, dst in (("q", sqs), ("k", sks)):
        p.dma(srow[:], qsrow if nm == "q" else ksrow)
        for n in range(2):
            p.op("tensor", lambda e, n=n: e.matmul(pb[:], sones[:], srow[:, n * 512:(n + 1) * 512], start=True, stop=True))
            p.op("vector", lambda e, n=n, dst=dst: e.tensor_copy(out=dst[:, n * 512:(n + 1) * 512], in_=pb[:]))
    sx = p.sb("sx", [128, D], F32)
    sxb = p.sb("sxb", [128, D], BF16)
    sxT = p.sb("sxT", [128, 128], BF16)
    ptr = p.ps("ptr", [128, 128], BF16)
    pqk = p.ps("pqk", [128, 512], F32)
    sqkv = p.sb("sqkv", [128, 3072], F32)
    stab1 = p.sb("stab1", [128, D], F32)
    stab2 = p.sb("stab2", [128, D], F32)
    ssq_ = p.sb("ssq_", [128, D], F32)
    snrm = p.sb("snrm", [128, 16], F32)
    srot = p.sb("srot", [128, D], F32)
    stmp = p.sb("stmp", [128, D], F32)
    sob = p.sb("sob", [128, D], BF16)
    sxTs = [p.sb(f"sxTs{c}", [128, 128], BF16) for c in range(8)]

    def norm_rot(src_off, scl, cst, sst, outb):
        v = sqkv[:, src_off:src_off + D]
        v3 = v.rearrange("p (h c) -> p h c", c=64)
        n3 = snrm[:].rearrange("p (h o) -> p h o", o=1)
        p.op("vector", lambda e, v=v: e.tensor_mul(out=ssq_[:], in0=v, in1=v))
        p.op("vector", lambda e: e.tensor_reduce(out=snrm[:], in_=ssq_[:].rearrange("p (h c) -> p h c", c=64), axis=AX.X, op=ALU.add))
        p.op("scalar", lambda e: e.sqrt(out=snrm[:], in_=snrm[:]))
        p.op("vector", lambda e: e.tensor_scalar(out=snrm[:], in0=snrm[:], scalar1=1e-12, scalar2=None, op0=ALU.max))
        p.op("vector", lambda e: e.reciprocal(out=snrm[:], in_=snrm[:]))
        p.op("vector", lambda e, v=v, v3=v3, n3=n3: e.tensor_tensor(out=v3, in0=v3, in1=n3.to_broadcast([128, 16, 64]), op=ALU.mult))
        p.op("vector", lambda e, v=v, scl=scl: e.tensor_mul(out=v, in0=v, in1=scl[:]))
        p.op("vector", lambda e, v=v, cst=cst: e.tensor_mul(out=srot[:], in0=v, in1=cst[:]))
        v3b = v.rearrange("p (h c) -> p h c", c=64)
        r3 = stmp[:].rearrange("p (h c) -> p h c", c=64)
        s3 = sst[:].rearrange("p (h c) -> p h c", c=64)
        o3 = srot[:].rearrange("p (h c) -> p h c", c=64)
        p.op("vector", lambda e, v3b=v3b, s3=s3, r3=r3: e.tensor_tensor(out=r3[:, :, 0:32], in0=v3b[:, :, 32:64], in1=s3[:, :, 0:32], op=ALU.mult))
        p.op("vector", lambda e, o3=o3, r3=r3: e.tensor_sub(out=o3[:, :, 0:32], in0=o3[:, :, 0:32], in1=r3[:, :, 0:32]))
        p.op("vector", lambda e, v3b=v3b, s3=s3, r3=r3: e.tensor_tensor(out=r3[:, :, 32:64], in0=v3b[:, :, 0:32], in1=s3[:, :, 32:64], op=ALU.mult))
        p.op("vector", lambda e, o3=o3, r3=r3: e.tensor_add(out=o3[:, :, 32:64], in0=o3[:, :, 32:64], in1=r3[:, :, 32:64]))
        p.op("scalar", lambda e, outb=outb: e.copy(out=outb[:], in_=srot[:]))

    for tt in range(NT):
        r0 = tt * 128
        p.dma(sx[:, 0:512], x[r0:r0 + 128, 0:512])
        if tt == 0:
            p.op("vector", lambda e: e.memset(sx[:, 512:1024], 0.0))
            p.dma(sx[1:128, 512:1024], x[0:127, 512:1024])
        else:
            p.dma(sx[:, 512:1024], x[r0 - 1:r0 + 127, 512:1024])
        p.op("scalar", lambda e: e.copy(out=sxb[:], in_=sx[:]))
        for c in range(8):
            p.op("tensor", lambda e, c=c: e.transpose(ptr[:], sxb[:, c * 128:(c + 1) * 128], s_eye[:]))
            p.op("vector", lambda e, c=c: e.tensor_copy(out=sxTs[c][:], in_=ptr[:]))
        for n in range(6):
            for c in range(8):
                p.op("tensor", lambda e, n=n, c=c: e.matmul(
                    pqk[:], sxTs[c][:], sW[c][:, n * 512:(n + 1) * 512], start=(c == 0), stop=(c == 7)))
            p.op("vector", lambda e, n=n: e.tensor_copy(out=sqkv[:, n * 512:(n + 1) * 512], in_=pqk[:]))
        p.dma(stab1[:], csq[r0:r0 + 128, :])
        p.dma(stab2[:], ssq[r0:r0 + 128, :])
        norm_rot(0, sqs, stab1, stab2, sob)
        for c in range(8):
            p.op("tensor", lambda e, c=c: e.transpose(ptr[:], sob[:, c * 128:(c + 1) * 128], s_eye[:]))
            p.op("vector", lambda e: e.tensor_copy(out=sxT[:], in_=ptr[:]))
            p.dma(qT[c * 128:(c + 1) * 128, r0:r0 + 128], sxT[:])
        p.dma(stab1[:], csk[r0:r0 + 128, :])
        p.dma(stab2[:], ssk[r0:r0 + 128, :])
        norm_rot(D, sks, stab1, stab2, sob)
        for c in range(8):
            p.op("tensor", lambda e, c=c: e.transpose(ptr[:], sob[:, c * 128:(c + 1) * 128], s_eye[:]))
            p.op("vector", lambda e: e.tensor_copy(out=sxT[:], in_=ptr[:]))
            p.dma(kT[c * 128:(c + 1) * 128, r0:r0 + 128], sxT[:])
        p.op("scalar", lambda e: e.copy(out=sob[:], in_=sqkv[:, 2048:3072]))
        p.dma(V[r0:r0 + 128, :], sob[:])
    return p.finalize()


def build_ffn1():
    p = Prog("ffn1")
    x = p.inp("x", [T, D], F32)
    grow = p.inp("grow", [1, D], F32)
    brow = p.inp("brow", [1, D], F32)
    W1 = p.inp("W1", [D, 5632], F32)
    b1col = p.inp("b1col", [128, 44], F32)
    onesf = p.inp("onesf", [1, 128], F32)
    eye = p.inp("eye", [128, 128], BF16)
    h2T = p.outp("h2T", [2816, T], BF16)
    add_eps(p)
    s_eye = p.sb("s_eye", [128, 128], BF16)
    p.dma(s_eye[:], eye)
    sW1f = p.sb("sW1f", [128, 2816], F32)
    sW1 = [p.sb(f"sW1_{c}", [128, 5632], BF16) for c in range(8)]
    for c in range(8):
        for hf in range(2):
            p.dma(sW1f[:], W1[c * 128:(c + 1) * 128, hf * 2816:(hf + 1) * 2816])
            p.op("scalar", lambda e, c=c, hf=hf: e.copy(out=sW1[c][:, hf * 2816:(hf + 1) * 2816], in_=sW1f[:]))
    sb1 = p.sb("sb1", [128, 44], F32)
    p.dma(sb1[:], b1col)
    sones = p.sb("sones", [1, 128], F32)
    p.dma(sones[:], onesf)
    srow = p.sb("srow", [1, D], F32)
    sg = p.sb("sg", [128, D], F32)
    sbb = p.sb("sbb", [128, D], F32)
    pbr = p.ps("pbr", [128, 512], F32)
    for nm, dst in (("g", sg), ("b", sbb)):
        p.dma(srow[:], grow if nm == "g" else brow)
        for n in range(2):
            p.op("tensor", lambda e, n=n: e.matmul(pbr[:], sones[:], srow[:, n * 512:(n + 1) * 512], start=True, stop=True))
            p.op("vector", lambda e, n=n, dst=dst: e.tensor_copy(out=dst[:, n * 512:(n + 1) * 512], in_=pbr[:]))
    sx = p.sb("sx", [128, D], F32)
    stmp = p.sb("stmp", [128, D], F32)
    sstat = p.sb("sstat", [128, 2], F32)
    sxn = p.sb("sxn", [128, D], F32)
    sxnb = p.sb("sxnb", [128, D], BF16)
    ptr = p.ps("ptr", [128, 128], BF16)
    sxnT = [p.sb(f"sxnT{c}", [128, 512], BF16) for c in range(8)]
    pha = p.ps("pha", [128, 512], F32)
    phg = p.ps("phg", [128, 512], F32)
    sta = p.sb("sta", [128, 512], F32)
    sgel = p.sb("sgel", [128, 512], F32)
    sh2 = p.sb("sh2", [128, 512], BF16)
    for g in range(8):
        for i in range(4):
            tt = g * 4 + i
            r0 = tt * 128
            p.dma(sx[:, 0:512], x[r0:r0 + 128, 0:512])
            if tt == 0:
                p.op("vector", lambda e: e.memset(sx[:, 512:1024], 0.0))
                p.dma(sx[1:128, 512:1024], x[0:127, 512:1024])
            else:
                p.dma(sx[:, 512:1024], x[r0 - 1:r0 + 127, 512:1024])
            ln_rows(p, sx, stmp, sstat, sxn, D)
            p.op("vector", lambda e: e.tensor_mul(out=sxn[:], in0=sxn[:], in1=sg[:]))
            p.op("vector", lambda e: e.tensor_add(out=sxn[:], in0=sxn[:], in1=sbb[:]))
            p.op("scalar", lambda e: e.copy(out=sxnb[:], in_=sxn[:]))
            for c in range(8):
                p.op("tensor", lambda e, c=c: e.transpose(ptr[:], sxnb[:, c * 128:(c + 1) * 128], s_eye[:]))
                p.op("vector", lambda e, i=i, c=c: e.tensor_copy(out=sxnT[c][:, i * 128:(i + 1) * 128], in_=ptr[:]))
        for fp in range(22):
            for c in range(8):
                p.op("tensor", lambda e, fp=fp, c=c: e.matmul(
                    pha[:], sW1[c][:, (2 * fp) * 128:(2 * fp) * 128 + 128],
                    sxnT[c][:], start=(c == 0), stop=(c == 7)))
            for c in range(8):
                p.op("tensor", lambda e, fp=fp, c=c: e.matmul(
                    phg[:], sW1[c][:, (2 * fp + 1) * 128:(2 * fp + 1) * 128 + 128],
                    sxnT[c][:], start=(c == 0), stop=(c == 7)))
            p.op("vector", lambda e, fp=fp: e.tensor_scalar(
                out=sta[:], in0=pha[:], scalar1=sb1[:, 2 * fp:2 * fp + 1], scalar2=None, op0=ALU.add))
            p.op("scalar", lambda e, fp=fp: e.activation(out=sgel[:], in_=phg[:], func=AF.Gelu, bias=sb1[:, 2 * fp + 1:2 * fp + 2]))
            p.op("vector", lambda e: e.tensor_tensor(out=sh2[:], in0=sta[:], in1=sgel[:], op=ALU.mult))
            p.dma(h2T[fp * 128:(fp + 1) * 128, g * 512:(g + 1) * 512], sh2[:])
    return p.finalize()


def build_add():
    p = Prog("add")
    a = p.inp("a", [T, D], F32)
    b = p.inp("b", [T, D], F32)
    o = p.outp("o", [T, D], F32)
    sa = p.sb("sa", [128, D], F32)
    sbt = p.sb("sbt", [128, D], F32)
    for tt in range(NT):
        r0 = tt * 128
        p.dma(sa[:], a[r0:r0 + 128, :])
        p.dma(sbt[:], b[r0:r0 + 128, :])
        p.op("vector", lambda e: e.tensor_add(out=sa[:], in0=sa[:], in1=sbt[:]))
        p.dma(o[r0:r0 + 128, :], sa[:])
    return p.finalize()


def rotary_consts():
    DH, WIN = 64, 256
    half = np.arange(0, DH, 2, dtype=np.float64)
    inv_freq = 1.0 / 10000.0 ** (half / DH)
    t = np.arange(T, dtype=np.float64)
    freqs = t[:, None] * inv_freq[None, :]
    freqs = np.concatenate([freqs, freqs], -1)
    base_scale = (half + 0.4 * DH) / (1.4 * DH)
    power = (t - T // 2) / (WIN // 2)
    scale = base_scale[None, :] ** power[:, None]
    scale = np.concatenate([scale, scale], -1)
    cos, sin = np.cos(freqs), np.sin(freqs)
    csq = np.tile((cos * scale), (1, 16)).astype(np.float32)
    ssq = np.tile((sin * scale), (1, 16)).astype(np.float32)
    csk = np.tile((cos / scale), (1, 16)).astype(np.float32)
    ssk = np.tile((sin / scale), (1, 16)).astype(np.float32)
    return csq, ssq, csk, ssk


def loc_masks():
    M = np.zeros((128, 2048), np.float32)
    j = np.arange(512)
    for var in range(4):
        p_ = np.arange(128)[:, None]
        it = p_ + (128 if var % 2 == 1 else 0)
        if var < 2:
            jb = j[None, :] - 256
            bad = (j[None, :] < 256) | (jb > it)
        else:
            prev = j[None, :] < 256
            bad_prev = prev & (j[None, :] < it - 256 + 256)
            bad_cur = (~prev) & ((j[None, :] - 256) > it)
            bad = bad_prev | bad_cur
        M[:, var * 512:(var + 1) * 512] = np.where(bad, -1e30, 0.0)
    return M


def l0_rope_tabs():
    halfi = np.arange(16, dtype=np.float64)
    theta = 1.0 / 10000.0 ** (2.0 * halfi / 32.0)
    tpos = np.arange(T, dtype=np.float64)[:, None] * theta[None, :]
    ct = np.tile(np.cos(tpos), (1, 8)).astype(np.float32)
    st = np.tile(np.sin(tpos), (1, 8)).astype(np.float32)
    jpos = np.arange(96, dtype=np.float64)[:, None] * theta[None, :]
    cj, sj = np.cos(jpos), np.sin(jpos)
    cs_slot = np.concatenate([cj, cj, sj, sj], axis=1).astype(BF)
    return ct, st, cs_slot


def l0_qt0_masks():
    mk = np.zeros((128, 128), np.float32)
    ma = np.full((128, 128), NEG, np.float32)
    for t in range(64, 128):
        for s in range(128):
            if s <= t - 64:
                mk[t, s] = 1.0
                ma[t, s] = 0.0
            elif t + 1 <= s <= 95:
                mk[t, s] = 0.0
                ma[t, s] = -1e30 - s * 1e24
    return mk, ma


DEBUG = {}


def _f32(a):
    return np.ascontiguousarray(a, dtype=np.float32)


def kernel(**inputs):
    I = {k: np.asarray(v) for k, v in inputs.items()}
    x_full = _f32(I["L0_attn_x"])

    Wcat1 = _f32(np.concatenate([I["L0_attn_W_dkv"], I["L0_attn_W_dq"], I["L0_attn_idx_wq"],
                                 I["L0_attn_idx_wk"], I["L0_attn_W_kr"],
                                 np.zeros((D, 32), np.float32)], axis=1))
    Wcat2 = _f32(np.concatenate([I["L0_attn_W_uq"], I["L0_attn_W_qr"]], axis=1))
    wukT = _f32(np.stack([np.concatenate([I["L0_attn_W_uk"][:, h * 64:(h + 1) * 64].T] * 2, axis=0) for h in range(8)]))
    wuv = _f32(np.stack([np.stack([I["L0_attn_W_uv"][c * 128:(c + 1) * 128, h * 64:(h + 1) * 64]
                                   for h in range(8)]) for c in range(2)]))
    wout = _f32(I["L0_attn_W_out"])
    idw = _f32(I["L0_attn_idx_w"]).reshape(1, 4)
    phiT = _f32(I["L0_ffn_phi_w"].T)
    phib = _f32(I["L0_ffn_phi_b"]).reshape(1, 2048)
    ew = _f32(I["L0_ffn_expert_w"])
    ebrep = _f32(np.repeat(np.asarray(I["L0_ffn_expert_b"]), 32, axis=0))
    Wcat3 = _f32(np.concatenate([I["L1_attn_W_dq"], I["L1_attn_W_dkv"]], axis=1))
    Wuq1 = _f32(I["L1_attn_W_uq"])
    Wuk1 = _f32(I["L1_attn_W_uk"])
    Wuv1 = _f32(I["L1_attn_W_uv"])
    Wo1 = _f32(I["L1_attn_W_o"])
    Wqkv = _f32(I["L1_ffn_Wqkv"])
    qs_full = _f32(I["L1_ffn_q_scale"])
    ks_full = _f32(I["L1_ffn_k_scale"])
    Wo6 = _f32(I["L1_ffn_Wo"])
    lng = _f32(I["L1_ffn_ln_g"])
    lnb = _f32(I["L1_ffn_ln_b"])
    W16 = _f32(I["L1_ffn_W1"])
    b16 = _f32(I["L1_ffn_b1"])
    W26 = _f32(I["L1_ffn_W2"])
    b26 = _f32(I["L1_ffn_b2"])
    DI = 2730
    DIP = 2816
    W1p, b1c, W2p = [], [], []
    for i in range(6):
        a = np.concatenate([W16[i][:, :DI], np.zeros((D, DIP - DI), np.float32)], 1)
        g = np.concatenate([W16[i][:, DI:], np.zeros((D, DIP - DI), np.float32)], 1)
        w1 = np.zeros((D, 2 * DIP), np.float32)
        b1a = np.concatenate([b16[i][:DI], np.zeros(DIP - DI, np.float32)])
        b1g = np.concatenate([b16[i][DI:], np.zeros(DIP - DI, np.float32)])
        b1i = np.zeros(2 * DIP, np.float32)
        for fp in range(22):
            w1[:, (2 * fp) * 128:(2 * fp) * 128 + 128] = a[:, fp * 128:(fp + 1) * 128]
            w1[:, (2 * fp + 1) * 128:(2 * fp + 1) * 128 + 128] = g[:, fp * 128:(fp + 1) * 128]
            b1i[(2 * fp) * 128:(2 * fp) * 128 + 128] = b1a[fp * 128:(fp + 1) * 128]
            b1i[(2 * fp + 1) * 128:(2 * fp + 1) * 128 + 128] = b1g[fp * 128:(fp + 1) * 128]
        W1p.append(_f32(w1))
        b1c.append(_f32(b1i.reshape(44, 128).T))
        W2p.append(_f32(np.concatenate([W26[i], np.zeros((DIP - DI, D), np.float32)], 0)))

    csq, ssq, csk, ssk = rotary_consts()
    masks = loc_masks()
    ct, st_, cs_slot = l0_rope_tabs()
    mk0, ma0 = l0_qt0_masks()
    onesf = ONES_F

    def bc(arr):
        return [arr] * NCORE

    pg_ln = prog("ln", build_ln)
    pg_l0p1 = prog("g_l0p1", build_gemm, "g_l0p1", 1024, 1152, True, False, True, False, False)
    pg_l0e = prog("l0e", build_l0e)
    pg_l0p2 = prog("g_l0p2", build_gemm, "g_l0p2", 512, 768, True, False, True, False, False)
    pg_l0e2 = prog("l0e2", build_l0e2)
    pg_l0tk = prog("l0tk", build_l0tk)
    pg_l0at = prog("l0at", build_l0at)
    pg_l0out = prog("l0out", build_l0out)
    pg_moea = prog("moea", build_moe_a)
    pg_moeb = prog("moeb", build_moe_b)
    pg_gA = prog("g_A", build_gemm, "g_A", 1024, 512, False, False, True, False, False)
    pg_gB = prog("g_B", build_gemm, "g_B", 256, 1024, False, False, True, False, False)
    pg_gBV = prog("g_BV", build_gemm, "g_BV", 256, 1024, False, True, False, False, False)
    pg_af = prog("attnfull", build_attn_full)
    pg_gWo = prog("g_Wo", build_gemm, "g_Wo", 1024, 1024, True, False, False, False, True)
    pg_qkv = prog("qkv", build_qkv)
    pg_al = prog("attnloc", build_attn_loc)
    pg_ffn1 = prog("ffn1", build_ffn1)
    pg_gW2 = prog("g_W2", build_gemm, "g_W2", 2816, 1024, True, False, False, True, True)
    pg_add = prog("add", build_add)

    x0 = [x_full[b] for b in range(NCORE)]

    def LN(xs):
        r = run(pg_ln, [{"x": xs[b], "eye": EYE_BF} for b in range(NCORE)], ["y", "yT", "ybf"])
        return r

    r = LN(x0)
    y0T = [r[b]["yT"] for b in range(NCORE)]
    r = run(pg_l0p1, [{"xT": y0T[b], "W": Wcat1, "eye": EYE_BF} for b in range(NCORE)], ["yf", "yT"])
    g1 = [r[b]["yf"] for b in range(NCORE)]
    g1T = [r[b]["yT"] for b in range(NCORE)]
    r = run(pg_l0e, [{"ckv": np.ascontiguousarray(g1[b][:, 0:256]), "kr": np.ascontiguousarray(g1[b][:, 1088:1120])} for b in range(NCORE)], ["kvx"])
    kvx = [r[b]["kvx"] for b in range(NCORE)]
    r = run(pg_l0p2, [{"xT": g1T[b][256:768], "W": Wcat2, "eye": EYE_BF} for b in range(NCORE)], ["yf", "yT"])
    g2 = [r[b]["yf"] for b in range(NCORE)]
    qcT = [r[b]["yT"][0:512] for b in range(NCORE)]
    r = run(pg_l0e2, [{"qr": np.ascontiguousarray(g2[b][:, 512:768]), "ctab": ct, "stab": st_, "eye": EYE_BF}
                      for b in range(NCORE)], ["qxet"])
    qxet = [r[b]["qxet"] for b in range(NCORE)]
    r = run(pg_l0tk, [{"qiT": g1T[b][768:1024], "kiT": g1T[b][1024:1088], "idw": idw,
                       "onesf": onesf, "mk0": mk0, "ma0": ma0, "eyef": EYE_F}
                      for b in range(NCORE)], ["idxT"])
    idxT = [r[b]["idxT"] for b in range(NCORE)]
    olt = [np.zeros((NT, 2, 128, 1024), BF) for _ in range(NCORE)]
    for sl in range(8):
        t0 = sl * 512
        tt0 = sl * 4
        r = run(pg_l0at, [{"kvx": kvx[b], "idxTs": np.ascontiguousarray(idxT[b][:, t0:t0 + 512]),
                           "qcTs": np.ascontiguousarray(qcT[b][:, t0:t0 + 512]),
                           "qxets": qxet[b][tt0:tt0 + 4], "wukT": wukT,
                           "cs_slot": cs_slot, "eye": EYE_BF} for b in range(NCORE)], ["olt"])
        for b in range(NCORE):
            olt[b][tt0:tt0 + 4] = r[b]["olt"]
    r = run(pg_l0out, [{"olt": olt[b], "wuv": wuv, "wout": wout, "x0": x0[b]}
                       for b in range(NCORE)], ["x1"])
    x1 = [r[b]["x1"] for b in range(NCORE)]
    DEBUG.update(g1=g1[0], g2=g2[0], kvx=kvx[0], idxT=idxT[0], olt=olt[0], x1=x1[0], y0T=y0T[0], qcT=qcT[0], qxet=qxet[0])

    r = LN(x1)
    y1T = [r[b]["yT"] for b in range(NCORE)]
    y1b = [r[b]["ybf"] for b in range(NCORE)]
    r = run(pg_moea, [{"yT": y1T[b], "yb": y1b[b], "phiT": phiT, "brow": phib, "onesf": onesf}
                      for b in range(NCORE)], ["logits", "xpart"])
    logits = [r[b]["logits"] for b in range(NCORE)]
    xparts = [r[b]["xpart"] for b in range(NCORE)]
    mb_in = []
    for b in range(NCORE):
        d = {f"xp{i}": xparts[i] for i in range(8)}
        d.update({"logits": logits[b], "ew": ew, "ebrep": ebrep, "x1": x1[b], "eye": EYE_BF})
        mb_in.append(d)
    r = run(pg_moeb, mb_in, ["x2"])
    x2 = [r[b]["x2"] for b in range(NCORE)]
    DEBUG.update(logits=logits[0], xparts=[x.copy() for x in xparts], x2=x2[0])

    r = LN(x2)
    y2T = [r[b]["yT"] for b in range(NCORE)]
    r = run(pg_gA, [{"xT": y2T[b], "W": Wcat3, "eye": EYE_BF} for b in range(NCORE)], ["yT"])
    g3T = [r[b]["yT"] for b in range(NCORE)]
    r = run(pg_gB, [{"xT": g3T[b][0:256], "W": Wuq1, "eye": EYE_BF} for b in range(NCORE)], ["yT"])
    QT = [r[b]["yT"] for b in range(NCORE)]
    r = run(pg_gB, [{"xT": g3T[b][256:512], "W": Wuk1, "eye": EYE_BF} for b in range(NCORE)], ["yT"])
    KT = [r[b]["yT"] for b in range(NCORE)]
    r = run(pg_gBV, [{"xT": g3T[b][256:512], "W": Wuv1, "eye": EYE_BF} for b in range(NCORE)], ["ybf"])
    V1 = [r[b]["ybf"] for b in range(NCORE)]
    aoT = [np.zeros((1024, T), BF) for _ in range(NCORE)]
    for hg in range(4):
        r = run(pg_af, [{"qT": QT[b][hg * 256:(hg + 1) * 256], "kT": KT[b][hg * 256:(hg + 1) * 256],
                         "V": np.ascontiguousarray(V1[b][:, hg * 256:(hg + 1) * 256]),
                         "eye": EYE_BF} for b in range(NCORE)], ["aoT"])
        for b in range(NCORE):
            aoT[b][hg * 256:(hg + 1) * 256] = r[b]["aoT"]
    r = run(pg_gWo, [{"xT": aoT[b], "W": Wo1, "res": x2[b], "eye": EYE_BF}
                     for b in range(NCORE)], ["yf"])
    x3 = [r[b]["yf"] for b in range(NCORE)]
    DEBUG.update(x3=x3[0], QT=QT[0], KT=KT[0], V1=V1[0], aoT=aoT[0])

    r = LN(x3)
    rstream = [r[b]["y"] for b in range(NCORE)]
    for i in range(6):
        qsrow = _f32(np.tile(qs_full[i], 16)).reshape(1, D)
        ksrow = _f32(np.tile(ks_full[i], 16)).reshape(1, D)
        r = run(pg_qkv, [{"x": rstream[b], "Wq": Wqkv[i], "qsrow": qsrow, "ksrow": ksrow,
                          "csq": csq, "ssq": ssq, "csk": csk, "ssk": ssk,
                          "onesf": onesf, "eye": EYE_BF} for b in range(NCORE)], ["qT", "kT", "V"])
        qTl = [r[b]["qT"] for b in range(NCORE)]
        kTl = [r[b]["kT"] for b in range(NCORE)]
        Vl = [r[b]["V"] for b in range(NCORE)]
        aoTl = [np.zeros((1024, T), BF) for _ in range(NCORE)]
        for hg in range(2):
            r = run(pg_al, [{"qT": qTl[b][hg * 512:(hg + 1) * 512], "kT": kTl[b][hg * 512:(hg + 1) * 512],
                             "V": np.ascontiguousarray(Vl[b][:, hg * 512:(hg + 1) * 512]),
                             "masks": masks, "eye": EYE_BF} for b in range(NCORE)], ["aoT"])
            for b in range(NCORE):
                aoTl[b][hg * 512:(hg + 1) * 512] = r[b]["aoT"]
        r = run(pg_gWo, [{"xT": aoTl[b], "W": Wo6[i], "res": rstream[b], "eye": EYE_BF}
                         for b in range(NCORE)], ["yf"])
        rstream = [r[b]["yf"] for b in range(NCORE)]
        grow = lng[i].reshape(1, D)
        brow = lnb[i].reshape(1, D)
        r = run(pg_ffn1, [{"x": rstream[b], "grow": grow, "brow": brow, "W1": W1p[i],
                           "b1col": b1c[i], "onesf": onesf, "eye": EYE_BF}
                          for b in range(NCORE)], ["h2T"])
        h2T = [r[b]["h2T"] for b in range(NCORE)]
        r = run(pg_gW2, [{"xT": h2T[b], "W": W2p[i], "brow": b26[i].reshape(1, D),
                          "onesf": onesf, "res": rstream[b], "eye": EYE_BF}
                         for b in range(NCORE)], ["yf"])
        rstream = [r[b]["yf"] for b in range(NCORE)]
        DEBUG[f"r{i}"] = rstream[0]
    r = run(pg_add, [{"a": x3[b], "b": rstream[b]} for b in range(NCORE)], ["o"])
    out = np.stack([r[b]["o"] for b in range(NCORE)]).astype(np.float32)
    return out
